# version 02 (final)
# baseline (speedup 1.0000x reference)
import numpy as np
import ml_dtypes
from contextlib import ExitStack

import concourse.bass as bass
import concourse.mybir as mybir
from concourse.bass_utils import run_bass_kernel_spmd

F32 = mybir.dt.float32
BF16 = mybir.dt.bfloat16
AF = mybir.ActivationFunctionType
ALU = mybir.AluOpType
AX = mybir.AxisListType
BFD = ml_dtypes.bfloat16
F16 = mybir.dt.float16

T = 4096
DM = 1024
CH = 512
NCH = T // CH
CONST_VALUES = [1e-5, 1e-30, 1.0 / 1024, -1.0, -20.0, 20.0, 0.125, 8.0, 1.0,
                0.0, 0.5, 1e-12, 2.0, -0.5, 0.25, 1.0 / 8, -1e9, 11.0]



def _rope_tables_L0():
    inv = 1.0 / 10000.0 ** (np.arange(0, 64, 2, dtype=np.float64) / 64)
    t = np.arange(T, dtype=np.float64)
    fr = np.outer(t, inv)
    cos = np.cos(fr).T.astype(np.float32)
    sin = np.sin(fr).T.astype(np.float32)
    cos_t = np.concatenate([cos, cos, cos, cos], 0)
    sin_t = np.concatenate([-sin, sin, -sin, sin], 0)
    return cos_t, sin_t


def _xpos_tables(window):
    dh = 64
    half = np.arange(0, dh, 2, dtype=np.float64)
    inv = 1.0 / 10000.0 ** (half / dh)
    t = np.arange(T, dtype=np.float64)
    fr = t[:, None] * inv[None, :]
    base = (half + 0.4 * dh) / (1.4 * dh)
    power = (t - T // 2) / (window // 2)
    sc = base[None, :] ** power[:, None]
    cos = np.cos(fr)
    sin = np.sin(fr)

    def pack(c, s):
        return (np.concatenate([c, c, c, c], 0).astype(np.float32),
                np.concatenate([-s, s, -s, s], 0).astype(np.float32))
    qcos, qsin = pack((cos * sc).T, (sin * sc).T)
    kcos, ksin = pack((cos / sc).T, (sin / sc).T)
    return qcos, qsin, kcos, ksin


def _mask_causal():
    m = np.zeros((128, 4 * CH), np.float32)
    pp = np.arange(128)[:, None]
    ff = np.arange(CH)[None, :]
    for j in range(4):
        m[:, j * CH:(j + 1) * CH] = (ff >= 128 * j + pp)
    return m.astype(BFD)


def _mask_window(window, nj):
    m = np.zeros((128, nj * CH), np.float32)
    pp = np.arange(128)[:, None]
    ff = np.arange(CH)[None, :]
    off = (nj - 4) * 128
    for j in range(nj):
        tk = 128 * j - off + pp
        m[:, j * CH:(j + 1) * CH] = (tk <= ff) & (tk >= ff - window)
    return m.astype(BFD)


def _perm_rope_L0(nheads):
    perm = []
    for h in range(nheads):
        perm += [h * 64 + 2 * j for j in range(32)]
        perm += [h * 64 + 2 * j + 1 for j in range(32)]
    return np.array(perm)


def _col128(v):
    n = v.shape[0] // 128
    return np.ascontiguousarray(v.reshape(n, 128).T.astype(np.float32))


def host_constants():
    c = {}
    c["c_ident"] = np.eye(128, dtype=BFD)
    c["c_identh"] = np.eye(128, dtype=np.float16)
    c["c_idf"] = np.eye(128, dtype=np.float32)
    c["c_onesb"] = np.ones((128, 1), BFD)
    c["c_onesf"] = np.ones((1, 128), np.float32)
    c["c_l0cos"], c["c_l0sin"] = _rope_tables_L0()
    for w in (256, 512):
        qc, qs, kc, ks = _xpos_tables(w)
        c[f"c_xq{w}c"], c[f"c_xq{w}s"] = qc, qs
        c[f"c_xk{w}c"], c[f"c_xk{w}s"] = kc, ks
    c["c_mcausal"] = _mask_causal()
    c["c_mwin256"] = _mask_window(256, 6)
    c["c_mwin512"] = _mask_window(512, 8)
    ob = np.zeros((128, 2), np.float32)
    ob[0:64, 0] = 1
    ob[64:128, 1] = 1
    c["c_oblk"] = ob.astype(BFD)
    c["c_oblkT"] = np.ascontiguousarray(ob.T)
    flts = np.zeros((128, 64), np.float32)
    flts[:, :len(CONST_VALUES)] = np.array(CONST_VALUES, np.float32)[None, :]
    c["c_flts"] = flts
    return c


def prep_weights(I):
    w = {}
    pq16 = _perm_rope_L0(16)
    pq4 = _perm_rope_L0(4)
    w["l0_wqa"] = I["L0_attn_Wqa"]
    w["l0_wqb"] = np.ascontiguousarray(I["L0_attn_Wqb"][:, pq16])
    w["l0_wkva"] = I["L0_attn_Wkva"]
    kvb = I["L0_attn_Wkvb"].reshape(32, 4, 64, 2)
    w["l0_wkb"] = np.ascontiguousarray(kvb[..., 0].reshape(32, 256)[:, pq4])
    w["l0_wvb"] = np.ascontiguousarray(kvb[..., 1].reshape(32, 256))
    w["l0_wout"] = I["L0_attn_Wout"]
    w["l0_bout"] = _col128(I["L0_attn_bout"])
    w["l0f_win"] = I["L0_ffn_Win"]
    w["l0f_bin"] = np.pad(I["L0_ffn_bin_"].reshape(64, 1).astype(np.float32), ((0, 64), (0, 0)))
    w["l0f_w1"] = I["L0_ffn_W1"].astype(np.float32)
    w["l0f_b1"] = I["L0_ffn_b1"].astype(np.float32)
    w["l0f_w2"] = I["L0_ffn_W2"].astype(np.float32)
    w["l0f_b2"] = np.pad(I["L0_ffn_b2"].reshape(64, 1).astype(np.float32), ((0, 64), (0, 0)))
    w["l0f_wout"] = I["L0_ffn_Wout"]
    w["l0f_bout"] = _col128(I["L0_ffn_bout"])
    for nm in ("Wq", "Wk", "Wv", "Wp"):
        w[f"l1_{nm.lower()}t"] = np.ascontiguousarray(I[f"L1_attn_{nm}"].T)
    w["l1_bq"] = _col128(I["L1_attn_bq"])
    w["l1_bk"] = _col128(I["L1_attn_bk"])
    w["l1_bv"] = _col128(I["L1_attn_bv"])
    w["l1_bp"] = _col128(I["L1_attn_bp"])
    for nm in ("lambda_q1", "lambda_k1", "lambda_q2", "lambda_k2"):
        w[f"l1_{nm}"] = np.ascontiguousarray(I[f"L1_attn_{nm}"].reshape(8, 64))

    def inner(pref, src, depth, heads, inner_dim):
        hid = 2730
        for i in range(depth):
            wqkv = src["Wqkv"][i]
            w[f"{pref}{i}_wq"] = np.ascontiguousarray(wqkv[:, :inner_dim])
            w[f"{pref}{i}_wk"] = np.ascontiguousarray(
                wqkv[:, inner_dim:2 * inner_dim])
            w[f"{pref}{i}_wv"] = np.ascontiguousarray(wqkv[:, 2 * inner_dim:])
            w[f"{pref}{i}_wo"] = src["Wo"][i]
            W1 = src["W1"][i]
            b1 = src["b1"][i]
            W1p = np.zeros((DM, 43 * 128), np.float32)
            b1p = np.zeros((43 * 128,), np.float32)
            for blk in range(43):
                a0 = 64 * blk
                an = min(64, hid - a0)
                W1p[:, 128 * blk:128 * blk + an] = W1[:, a0:a0 + an]
                W1p[:, 128 * blk + 64:128 * blk + 64 + an] = W1[:, hid + a0:hid + a0 + an]
                b1p[128 * blk:128 * blk + an] = b1[a0:a0 + an]
                b1p[128 * blk + 64:128 * blk + 64 + an] = b1[hid + a0:hid + a0 + an]
            w[f"{pref}{i}_w1p"] = W1p
            w[f"{pref}{i}_b1p"] = _col128(b1p)
            W2p = np.zeros((2752, DM), np.float32)
            W2p[:hid] = src["W2"][i]
            w[f"{pref}{i}_w2p"] = W2p
            w[f"{pref}{i}_b2"] = _col128(src["b2"][i])
            w[f"{pref}{i}_lng"] = _col128(src["ln_g"][i])
            w[f"{pref}{i}_lnb"] = _col128(src["ln_b"][i])
            qs = src["q_scale"][i].astype(np.float32)
            ks = src["k_scale"][i].astype(np.float32)
            w[f"{pref}{i}_qsc"] = np.tile(qs, 2).reshape(128, 1)
            w[f"{pref}{i}_qscs"] = np.tile(
                np.concatenate([qs[32:64], qs[0:32]]), 2).reshape(128, 1)
            w[f"{pref}{i}_ksc"] = np.tile(ks, 2).reshape(128, 1)
            w[f"{pref}{i}_kscs"] = np.tile(
                np.concatenate([ks[32:64], ks[0:32]]), 2).reshape(128, 1)

    inner("l1f", dict(Wqkv=I["L1_ffn_Wqkv"], Wo=I["L1_ffn_Wo"], W1=I["L1_ffn_W1"],
                      b1=I["L1_ffn_b1"], W2=I["L1_ffn_W2"], b2=I["L1_ffn_b2"],
                      ln_g=I["L1_ffn_ln_g"], ln_b=I["L1_ffn_ln_b"],
                      q_scale=I["L1_ffn_q_scale"], k_scale=I["L1_ffn_k_scale"]),
          6, 16, 1024)
    inner("l3f", dict(Wqkv=I["L3_ffn_wqkv"], Wo=I["L3_ffn_wo"], W1=I["L3_ffn_w1"],
                      b1=I["L3_ffn_b1"], W2=I["L3_ffn_w2"], b2=I["L3_ffn_b2"],
                      ln_g=I["L3_ffn_ln_g"], ln_b=I["L3_ffn_ln_b"],
                      q_scale=I["L3_ffn_q_scale"], k_scale=I["L3_ffn_k_scale"]),
          2, 8, 512)
    w["l2_wkp"] = I["L2_attn_W_K"]
    w["l2_wqp"] = I["L2_attn_W_Q"]
    w["l2_wvp"] = I["L2_attn_W_V"]
    w["l2_wpred"] = I["L2_attn_W_pred"]
    w["l2_wr"] = I["L2_ffn_router_kernel_DE"]
    w["l2_wrp"] = np.ascontiguousarray(
        I["L2_ffn_router_kernel_DE"].reshape(8, 128, 8).transpose(1, 0, 2)
        .reshape(128, 64))
    for e in range(8):
        w[f"l2_wg{e}"] = np.ascontiguousarray(I["L2_ffn_kernel_gating_EDF"][e])
        w[f"l2_wu{e}"] = np.ascontiguousarray(I["L2_ffn_kernel_up_EDF"][e])
        w[f"l2_wd{e}"] = np.ascontiguousarray(I["L2_ffn_kernel_down_EFD"][e])
    w["l2_wpredr"] = np.ascontiguousarray(
        I["L2_attn_W_pred"].transpose(1, 0, 2).reshape(64, 16 * 64))
    for nm in ("Wq", "Wk", "Wv", "Wo"):
        w[f"l3_{nm.lower()}t"] = np.ascontiguousarray(I[f"L3_attn_{nm}"].T)
    return w



class Sem:
    __slots__ = ("h", "n")

    def __init__(self, h):
        self.h = h
        self.n = 0

    def inc(self, ins, k=1):
        ins.then_inc(self.h, k)
        self.n += k
        return self.n


class P:
    def __init__(self, nc, ctx):
        self.nc = nc
        self.ctx = ctx
        self.d = {}
        self.psum_ctr = 0
        self.warena = self.sbuf("warena", [128, 16896], BF16)
        self.abuf = self.sbuf("abuf", [128, 15360], BF16)
        self.stbf = self.sbuf("stbf", [128, 8192], BF16)
        self.stf = self.sbuf("stf", [128, 8192], F32)
        self.fbuf = self.sbuf("fbuf", [128, 9216], F32)
        self.small = self.sbuf("small", [128, 2048], F32)
        self.maskc = self.sbuf("maskc", [128, 9216], BF16)
        self.ident = self.sbuf("identc", [128, 128], BF16)
        self.identh = self.sbuf("identhc", [128, 128], F16)
        self.onesb = self.sbuf("onesbc", [128, 1], BF16)
        self.onesf = self.sbuf("onesfc", [1, 128], F32)
        self.oblk = self.sbuf("oblkc", [128, 2], BF16)
        self.oblkT = self.sbuf("oblkTc", [2, 128], F32)
        self.cflt = self.sbuf("cfltc", [128, 64], F32)
        self.smallb = self.sbuf("smallb", [128, 16], BF16)
        for i, v in enumerate(CONST_VALUES):
            nc.const_aps.register(F32, v, self.cflt[:, i:i + 1])
        self.ksb = self.sbuf("ksb", [64, T], BF16)
        self.vsb = self.sbuf("vsb", [64, T], BF16)
        self.vtok = self.sbuf("vtok", [128, 32 * 65], BF16)
        self.qsb = self.sbuf("qsb", [64, 2 * CH], BF16)
        self.psb = self.sbuf("psb", [128, 4 * CH], BF16)
        self.pmk = self.sbuf("pmk", [128, 2 * CH], BF16)
        for nm in ["LW", "LS", "LA", "MM", "EV", "SS", "VS", "ST", "NM", "AX", "GP", "LSB", "LWB", "KV", "QS1", "QS2"]:
            setattr(self, nm, Sem(ctx.enter_context(nc.semaphore(nm))))

    def sbuf(self, n, s, d):
        return self.ctx.enter_context(self.nc.sbuf_tensor(n, s, d))

    def dram(self, name, shape, dtype, kind="Internal"):
        h = self.nc.dram_tensor(name, list(shape), dtype, kind=kind)
        self.d[name] = h
        return h

    def psum(self, stack, shape=(128, 512), dtype=F32):
        self.psum_ctr += 1
        return stack.enter_context(
            self.nc.psum_tensor(f"ps{self.psum_ctr}", list(shape), dtype))


def cdiv(a, b):
    return (a + b - 1) // b



def st_init(p):
    nc = p.nc
    with nc.Block() as b:
        @b.gpsimd
        def _(g):
            p.GP.inc(g.dma_start(out=p.d["XOUT"][:, :], in_=p.d["XT0"][:, :]), 16)
            loads = [
                (p.ident[:, :], p.d["c_ident"][:, :]),
                (p.identh[:, :], p.d["c_identh"][:, :]),
                (p.onesb[:, :], p.d["c_onesb"][:, :]),
                (p.onesf[:, :], p.d["c_onesf"][:, :]),
                (p.oblk[:, :], p.d["c_oblk"][:, :]),
                (p.oblkT[:, :], p.d["c_oblkT"][:, :]),
                (p.maskc[:, 0:2048], p.d["c_mcausal"][:, :]),
                (p.maskc[:, 2048:5120], p.d["c_mwin256"][:, :]),
                (p.maskc[:, 5120:9216], p.d["c_mwin512"][:, :]),
                (p.cflt[:, :], p.d["c_flts"][:, :]),
            ]
            for o, i in loads:
                p.GP.inc(g.dma_start(out=o, in_=i), 16)
            g.wait_ge(p.GP.h, p.GP.n)


def st_layernorm(p, src_name, dst_bf, dst_f32=None, gain=None, beta=None,
                 shifted=False, eps=1e-5):
    nc = p.nc
    src = p.d[src_name]
    dual = dst_f32 is not None
    has_gain = gain is not None
    ssper = 4 if dual else 3
    with ExitStack() as ps_ctx, nc.Block() as b:
        st1 = p.psum(ps_ctx, (1, 512))
        st2 = p.psum(ps_ctx, (1, 512))
        bc1 = p.psum(ps_ctx)
        bc2 = p.psum(ps_ctx)
        lsp = [p.LS, p.LSB]
        LSP0 = [p.LS.n, p.LSB.n]
        SS0, VS0, MM0, ST0, LA0 = p.SS.n, p.VS.n, p.MM.n, p.ST.n, p.LA.n
        inv = 1.0 / DM
        stores_per_c = 16 * 8 * (2 if dual else 1)

        @b.gpsimd
        def _(g):
            if has_gain:
                p.LA.inc(g.dma_start(out=p.fbuf[:, 8192:8200], in_=p.d[gain][:, :]), 16)
                p.LA.inc(g.dma_start(out=p.fbuf[:, 8200:8208], in_=p.d[beta][:, :]), 16)
            for c in range(NCH):
                if c >= 1:
                    g.wait_ge(p.VS.h, VS0 + 4 * c)
                for k in range(8):
                    if shifted and k >= 4:
                        if c == 0:
                            g.memset(p.fbuf[:, k * CH:k * CH + 1], 0.0)
                            ins = g.dma_start(
                                out=p.fbuf[:, k * CH + 1:(k + 1) * CH],
                                in_=src[k * 128:(k + 1) * 128, 0:CH - 1])
                        else:
                            ins = g.dma_start(
                                out=p.fbuf[:, k * CH:(k + 1) * CH],
                                in_=src[k * 128:(k + 1) * 128, c * CH - 1:(c + 1) * CH - 1])
                    else:
                        ins = g.dma_start(
                            out=p.fbuf[:, k * CH:(k + 1) * CH],
                            in_=src[k * 128:(k + 1) * 128, c * CH:(c + 1) * CH])
                    lsp[c % 2].inc(ins, 16)

        @b.tensor
        def _(t):
            for c in range(NCH):
                t.wait_ge(p.SS.h, SS0 + ssper * c + 1)
                if c >= 1:
                    t.wait_ge(p.VS.h, VS0 + 4 * (c - 1) + 1)
                for k in range(8):
                    t.matmul(st1[:, :], p.onesb[:, :], p.abuf[:, k * CH:(k + 1) * CH],
                             start=(k == 0), stop=(k == 7))
                for k in range(8):
                    ins = t.matmul(st2[:, :], p.onesb[:, :],
                                   p.abuf[:, 4096 + k * CH:4096 + (k + 1) * CH],
                                   start=(k == 0), stop=(k == 7))
                p.MM.inc(ins)
                t.wait_ge(p.VS.h, VS0 + 4 * c + 3)
                t.matmul(bc1[:, :], p.onesf[:, :], p.small[0:1, 1024:1536],
                         start=True, stop=True)
                ins = t.matmul(bc2[:, :], p.onesf[:, :], p.small[0:1, 1536:2048],
                               start=True, stop=True)
                p.MM.inc(ins)

        @b.scalar
        def _(s):
            if has_gain:
                s.wait_ge(p.LA.h, LA0 + 32)
            for c in range(NCH):
                s.wait_ge(lsp[c % 2].h, LSP0[c % 2] + 16 * 8 * (c // 2 + 1))
                if c >= 1:
                    s.wait_ge(p.MM.h, MM0 + 2 * (c - 1) + 1)
                for k in range(8):
                    s.copy(p.abuf[:, k * CH:(k + 1) * CH], p.fbuf[:, k * CH:(k + 1) * CH])
                for k in range(8):
                    ins = s.activation(p.abuf[:, 4096 + k * CH:4096 + (k + 1) * CH],
                                       p.fbuf[:, k * CH:(k + 1) * CH], AF.Square)
                p.SS.inc(ins)
                s.wait_ge(p.VS.h, VS0 + 4 * c + 1)
                ins = s.activation(p.small[0:1, 1536:2048], p.small[0:1, 0:512],
                                   AF.Square)
                p.SS.inc(ins)
                s.wait_ge(p.VS.h, VS0 + 4 * c + 2)
                ins = s.activation(p.small[0:1, 1024:1536], p.small[0:1, 512:1024],
                                   AF.Sqrt, bias=eps)
                p.SS.inc(ins)
                if dual:
                    s.wait_ge(p.VS.h, VS0 + 4 * c + 4)
                    for k in range(8):
                        ins = s.copy(
                            p.stbf[:, ((c % 2) * 8 + k) * CH:((c % 2) * 8 + k + 1) * CH],
                            p.stf[:, k * CH:(k + 1) * CH])
                    p.SS.inc(ins)

        @b.vector
        def _(v):
            for c in range(NCH):
                v.wait_ge(p.MM.h, MM0 + 2 * c + 1)
                v.tensor_scalar_mul(p.small[0:1, 0:512], st1[:, :], inv)
                ins = v.tensor_scalar_mul(p.small[0:1, 512:1024], st2[:, :], inv)
                p.VS.inc(ins)
                v.wait_ge(p.SS.h, SS0 + ssper * c + 2)
                ins = v.tensor_sub(p.small[0:1, 512:1024], p.small[0:1, 512:1024],
                                   p.small[0:1, 1536:2048])
                p.VS.inc(ins)
                v.wait_ge(p.SS.h, SS0 + ssper * c + 3)
                v.reciprocal(p.small[0:1, 1024:1536], p.small[0:1, 1024:1536])
                v.tensor_mul(p.small[0:1, 1536:2048], p.small[0:1, 0:512],
                             p.small[0:1, 1024:1536])
                ins = v.tensor_scalar_mul(p.small[0:1, 1536:2048],
                                          p.small[0:1, 1536:2048], -1.0)
                p.VS.inc(ins)
                v.wait_ge(p.MM.h, MM0 + 2 * c + 2)
                if c >= 2:
                    v.wait_ge(p.ST.h, ST0 + stores_per_c * c)
                if dual and c >= 1:
                    v.wait_ge(p.ST.h, ST0 + stores_per_c * c)
                ins = None
                for k in range(8):
                    tmp = p.stf[:, 4096 + (k % 2) * CH:4096 + (k % 2 + 1) * CH]
                    v.tensor_mul(tmp, p.fbuf[:, k * CH:(k + 1) * CH], bc1[:, :])
                    if dual:
                        out_slot = p.stf[:, k * CH:(k + 1) * CH]
                    else:
                        out_slot = p.stbf[:, ((c % 2) * 8 + k) * CH:
                                          ((c % 2) * 8 + k + 1) * CH]
                    if has_gain:
                        v.tensor_add(tmp, tmp, bc2[:, :])
                        ins = v.tensor_scalar(out_slot, tmp,
                                              p.fbuf[:, 8192 + k:8193 + k],
                                              p.fbuf[:, 8200 + k:8201 + k],
                                              ALU.mult, ALU.add)
                    else:
                        ins = v.tensor_add(out_slot, tmp, bc2[:, :])
                p.VS.inc(ins)

        @b.sync
        def _(sy):
            for c in range(NCH):
                if dual:
                    sy.wait_ge(p.SS.h, SS0 + ssper * c + 4)
                else:
                    sy.wait_ge(p.VS.h, VS0 + 4 * c + 4)
                h_bf, off = dst_bf
                for k in range(8):
                    ins = sy.dma_start(
                        out=p.d[h_bf][off + k * 128:off + (k + 1) * 128,
                                      c * CH:(c + 1) * CH],
                        in_=p.stbf[:, ((c % 2) * 8 + k) * CH:((c % 2) * 8 + k + 1) * CH])
                    p.ST.inc(ins, 16)
                if dual:
                    h_f, off2 = dst_f32
                    for k in range(8):
                        ins = sy.dma_start(
                            out=p.d[h_f][off2 + k * 128:off2 + (k + 1) * 128,
                                         c * CH:(c + 1) * CH],
                            in_=p.stf[:, k * CH:(k + 1) * CH])
                        p.ST.inc(ins, 16)
            sy.wait_ge(p.ST.h, p.ST.n)


def st_linear(p, src, K, W, N, dst, dst_dtype=BF16, bias=None, act=None,
              evac="copy", rope=None, mul_other=None, src_f32=False,
              wrow=None, waccum=False):
    nc = p.nc
    src_h, src_off = p.d[src[0]], src[1]
    W_h, W_roff, W_coff = p.d[W[0]], W[1], W[2]
    dst_h, dst_off = p.d[dst[0]], dst[1]
    kt = cdiv(K, 128)
    nt = cdiv(N, 128)
    NG = 6
    ngroups = cdiv(nt, NG)
    f32_out = dst_dtype == F32
    with ExitStack() as ps_ctx, nc.Block() as b:
        ps = [p.psum(ps_ctx) for _ in range(min(nt, NG))]
        psW = p.psum(ps_ctx) if evac == "wdown" else None
        lsp = [p.LS, p.LSB]
        lwp = [p.LW, p.LWB]
        LSP0 = [p.LS.n, p.LSB.n]
        LWP0 = [p.LW.n, p.LWB.n]
        MM0, EV0, ST0, LA0 = p.MM.n, p.EV.n, p.ST.n, p.LA.n
        iters = []
        for g in range(ngroups):
            ng = min(NG, nt - g * NG)
            for c in range(NCH):
                iters.append((g, c, ng))
        n_iter = len(iters)
        tiles_w = [min(128, N - 128 * (g * NG + j)) for g in range(ngroups)
                   for j in range(NG)]

        @b.gpsimd
        def _(g_):
            if bias is not None:
                p.LA.inc(g_.dma_start(out=p.fbuf[:, 8192:8192 + nt],
                                      in_=p.d[bias][:, :]), 16)
            if rope is not None:
                p.LA.inc(g_.dma_start(out=p.fbuf[:, 0:4096],
                                      in_=p.d[rope[0]][:, :]), 16)
                p.LA.inc(g_.dma_start(out=p.fbuf[:, 4096:8192],
                                      in_=p.d[rope[1]][:, :]), 16)
            it = 0
            for g in range(ngroups):
                ng = min(NG, nt - g * NG)
                if g >= 1:
                    g_.wait_ge(p.MM.h, MM0 + g * NCH)
                for k in range(kt):
                    kr = min(128, K - 128 * k)
                    cw = sum(min(128, N - 128 * (g * NG + j)) for j in range(ng))
                    ins = g_.dma_start(
                        out=p.warena[0:kr, k * 768:k * 768 + cw],
                        in_=W_h[W_roff + 128 * k:W_roff + 128 * k + kr,
                                W_coff + 128 * g * NG:W_coff + 128 * g * NG + cw])
                    lwp[g % 2].inc(ins, 16)
                for c in range(NCH):
                    if it >= 1:
                        g_.wait_ge(p.MM.h, MM0 + it)
                    for k in range(kt):
                        kr = min(128, K - 128 * k)
                        ins = g_.dma_start(
                            out=p.abuf[0:kr, k * CH:(k + 1) * CH],
                            in_=src_h[src_off + 128 * k:src_off + 128 * k + kr,
                                      c * CH:(c + 1) * CH])
                        lsp[it % 2].inc(ins, 16)
                    if mul_other is not None:
                        oh, ooff = p.d[mul_other[0]], mul_other[1]
                        for j in range(ng):
                            n0 = 128 * (g * NG + j)
                            nw = min(128, N - n0)
                            ins = g_.dma_start(
                                out=p.abuf[0:nw, 12288 + j * CH:12288 + (j + 1) * CH],
                                in_=oh[ooff + n0:ooff + n0 + nw, c * CH:(c + 1) * CH])
                            lsp[it % 2].inc(ins, 16)
                    if evac == "wdown":
                        ins = g_.dma_start(
                            out=p.fbuf[0:1, 4096 + (c % 2) * CH:
                                       4096 + (c % 2 + 1) * CH],
                            in_=p.d[wrow[0]][wrow[1]:wrow[1] + 1,
                                             c * CH:(c + 1) * CH])
                        lsp[it % 2].inc(ins, 16)
                        if waccum:
                            for j in range(ng):
                                n0 = 128 * (g * NG + j)
                                nw = min(128, N - n0)
                                ins = g_.dma_start(
                                    out=p.fbuf[0:nw, j * CH:(j + 1) * CH],
                                    in_=dst_h[dst_off + n0:dst_off + n0 + nw,
                                              c * CH:(c + 1) * CH])
                                lsp[it % 2].inc(ins, 16)
                    it += 1

        ls_tgts = []
        accp = [LSP0[0], LSP0[1]]
        for it, (g, c, ng) in enumerate(iters):
            cnt = kt + (ng if mul_other is not None else 0)
            if evac == "wdown":
                cnt += 1 + (ng if waccum else 0)
            accp[it % 2] += 16 * cnt
            ls_tgts.append(accp[it % 2])
        lw_tgts = []
        accp = [LWP0[0], LWP0[1]]
        for g in range(ngroups):
            accp[g % 2] += 16 * kt
            lw_tgts.append(accp[g % 2])

        @b.tensor
        def _(t):
            for it, (g, c, ng) in enumerate(iters):
                t.wait_ge(lwp[g % 2].h, lw_tgts[g])
                t.wait_ge(lsp[it % 2].h, ls_tgts[it])
                if it >= 1:
                    t.wait_ge(p.EV.h, EV0 + it)
                for k in range(kt):
                    kr = min(128, K - 128 * k)
                    cw0 = 0
                    for j in range(ng):
                        nw = min(128, N - 128 * (g * NG + j))
                        ins = t.matmul(
                            ps[j][0:nw, :],
                            p.warena[0:kr, k * 768 + cw0:k * 768 + cw0 + nw],
                            p.abuf[0:kr, k * CH:(k + 1) * CH],
                            start=(k == 0), stop=(k == kt - 1),
                            skip_group_check=True)
                        cw0 += nw
                p.MM.inc(ins)
                if evac == "wdown":
                    ins = t.matmul(psW[:, :], p.onesf[:, :],
                                   p.fbuf[0:1, 4096 + (c % 2) * CH:
                                          4096 + (c % 2 + 1) * CH],
                                   start=True, stop=True, skip_group_check=True)
                    p.NM.inc(ins)

        cum_e = []
        _acc = 0
        for (_g, _c, _ng) in iters:
            cum_e.append(_acc)
            _acc += _ng

        def slot_of(it, j):
            return (cum_e[it] + j) % 16

        if evac in ("copy",):
            @b.scalar
            def _(s):
                if bias is not None:
                    s.wait_ge(p.LA.h, p.LA.n)
                for it, (g, c, ng) in enumerate(iters):
                    s.wait_ge(p.MM.h, MM0 + it + 1)
                    e0 = cum_e[it]
                    if e0 + ng > 16:
                        s.wait_ge(p.ST.h, ST0 + 16 * e0)
                    ins = None
                    for j in range(ng):
                        n_idx = g * NG + j
                        nw = min(128, N - 128 * n_idx)
                        slot = slot_of(it, j)
                        if f32_out:
                            out_ap = p.stf[0:nw, slot * CH:(slot + 1) * CH]
                        else:
                            out_ap = p.stbf[0:nw, slot * CH:(slot + 1) * CH]
                        bias_ap = (p.fbuf[0:nw, 8192 + n_idx:8193 + n_idx]
                                   if bias is not None else None)
                        if act is not None:
                            ins = s.activation(out_ap, ps[j][0:nw, :], act,
                                               bias=(bias_ap if bias_ap is not None
                                                     else 0.0))
                        elif bias is not None:
                            ins = s.add(out_ap, ps[j][0:nw, :], bias_ap)
                        else:
                            ins = s.copy(out_ap, ps[j][0:nw, :])
                    p.EV.inc(ins)
        elif evac == "rope":
            @b.vector
            def _(v):
                for it, (g, c, ng) in enumerate(iters):
                    v.wait_ge(p.MM.h, MM0 + it + 1)
                    if rope is not None and it == 0:
                        v.wait_ge(p.LA.h, p.LA.n)
                    e0 = cum_e[it]
                    if e0 + ng > 16:
                        v.wait_ge(p.ST.h, ST0 + 16 * e0)
                    ins = None
                    for j in range(ng):
                        slot = slot_of(it, j)
                        cosp = p.fbuf[:, c * CH:(c + 1) * CH]
                        sinp = p.fbuf[:, 4096 + c * CH:4096 + (c + 1) * CH]
                        tA = p.stf[:, 0:CH]
                        tB = p.stf[:, CH:2 * CH]
                        v.tensor_mul(tA, ps[j][:, :], cosp)
                        v.tensor_mul(tB[0:32, :], ps[j][32:64, :], sinp[0:32, :])
                        v.tensor_mul(tB[32:64, :], ps[j][0:32, :], sinp[32:64, :])
                        v.tensor_mul(tB[64:96, :], ps[j][96:128, :], sinp[64:96, :])
                        v.tensor_mul(tB[96:128, :], ps[j][64:96, :], sinp[96:128, :])
                        ins = v.tensor_add(p.stbf[:, slot * CH:(slot + 1) * CH], tA, tB)
                    p.EV.inc(ins)
        elif evac == "glu":
            @b.scalar
            def _(s):
                if bias is not None:
                    s.wait_ge(p.LA.h, p.LA.n)
                for it, (g, c, ng) in enumerate(iters):
                    s.wait_ge(p.MM.h, MM0 + it + 1)
                    ins = None
                    for j in range(ng):
                        bias_col = p.fbuf[:, 8192 + g * NG + j:8193 + g * NG + j]
                        s.add(p.stf[0:64, (2 * j) * CH:(2 * j + 1) * CH],
                              ps[j][0:64, :], bias_col[0:64, :])
                        ins = s.activation(
                            p.stf[0:64, (2 * j + 1) * CH:(2 * j + 2) * CH],
                            ps[j][64:128, :], AF.Gelu, bias=bias_col[64:128, :])
                    p.SS.inc(ins)

            @b.vector
            def _(v):
                SS0g = p.SS.n - n_iter
                for it, (g, c, ng) in enumerate(iters):
                    v.wait_ge(p.SS.h, SS0g + it + 1)
                    e0 = cum_e[it]
                    if e0 + ng > 16:
                        v.wait_ge(p.ST.h, ST0 + 16 * e0)
                    ins = None
                    for j in range(ng):
                        slot = slot_of(it, j)
                        ins = v.tensor_mul(
                            p.stbf[0:64, slot * CH:(slot + 1) * CH],
                            p.stf[0:64, (2 * j) * CH:(2 * j + 1) * CH],
                            p.stf[0:64, (2 * j + 1) * CH:(2 * j + 2) * CH])
                    p.EV.inc(ins)
        elif evac == "wdown":
            NM0w = p.NM.n - n_iter

            @b.vector
            def _(v):
                for it, (g, c, ng) in enumerate(iters):
                    v.wait_ge(p.MM.h, MM0 + it + 1)
                    v.wait_ge(p.NM.h, NM0w + it + 1)
                    if it >= 1:
                        v.wait_ge(p.ST.h, ST0 + 16 * cum_e[it])
                    ins = None
                    for j in range(ng):
                        nw = min(128, N - 128 * (g * NG + j))
                        a_sl = p.stf[0:nw, (2 * j) * CH:(2 * j + 1) * CH]
                        o_sl = p.stf[0:nw, (2 * j + 1) * CH:(2 * j + 2) * CH]
                        v.tensor_copy(a_sl, ps[j][0:nw, :])
                        ins = v.tensor_mul(o_sl, a_sl, psW[0:nw, :])
                        if waccum:
                            ins = v.tensor_add(o_sl, o_sl,
                                               p.fbuf[0:nw, j * CH:(j + 1) * CH])
                    p.EV.inc(ins)
        elif evac == "mulwith":
            @b.vector
            def _(v):
                for it, (g, c, ng) in enumerate(iters):
                    v.wait_ge(p.MM.h, MM0 + it + 1)
                    e0 = cum_e[it]
                    if e0 + ng > 16:
                        v.wait_ge(p.ST.h, ST0 + 16 * e0)
                    ins = None
                    for j in range(ng):
                        n_idx = g * NG + j
                        nw = min(128, N - 128 * n_idx)
                        slot = slot_of(it, j)
                        if act is not None:
                            pass
                        ins = v.tensor_mul(
                            p.stbf[0:nw, slot * CH:(slot + 1) * CH],
                            ps[j][0:nw, :],
                            p.abuf[0:nw, 12288 + j * CH:12288 + (j + 1) * CH])
                    p.EV.inc(ins)

        @b.sync
        def _(sy):
            for it, (g, c, ng) in enumerate(iters):
                sy.wait_ge(p.EV.h, EV0 + it + 1)
                for j in range(ng):
                    n_idx = g * NG + j
                    slot = slot_of(it, j)
                    if evac == "wdown":
                        n0 = 128 * n_idx
                        nw = min(128, N - n0)
                        ins = sy.dma_start(
                            out=dst_h[dst_off + n0:dst_off + n0 + nw,
                                      c * CH:(c + 1) * CH],
                            in_=p.stf[0:nw, (2 * j + 1) * CH:(2 * j + 2) * CH])
                        p.ST.inc(ins, 16)
                        continue
                    if evac == "glu":
                        r0 = dst_off + 64 * n_idx
                        ins = sy.dma_start(
                            out=dst_h[r0:r0 + 64, c * CH:(c + 1) * CH],
                            in_=p.stbf[0:64, slot * CH:(slot + 1) * CH])
                    else:
                        n0 = 128 * n_idx
                        nw = min(128, N - n0)
                        if f32_out:
                            src_ap = p.stf[0:nw, slot * CH:(slot + 1) * CH]
                        else:
                            src_ap = p.stbf[0:nw, slot * CH:(slot + 1) * CH]
                        ins = sy.dma_start(
                            out=dst_h[dst_off + n0:dst_off + n0 + nw,
                                      c * CH:(c + 1) * CH],
                            in_=src_ap)
                    p.ST.inc(ins, 16)
            sy.wait_ge(p.ST.h, p.ST.n)


def st_add(p, dst, srcs):
    nc = p.nc
    dst_h = p.d[dst]
    nsrc = len(srcs)
    with nc.Block() as b:
        lsp = [p.LS, p.LSB]
        LSP0 = [p.LS.n, p.LSB.n]
        VS0, ST0 = p.VS.n, p.ST.n

        @b.gpsimd
        def _(g):
            for c in range(NCH):
                if c >= 1:
                    g.wait_ge(p.VS.h, VS0 + 8 * c)
                for k in range(8):
                    ins = g.dma_start(out=p.fbuf[:, k * CH:(k + 1) * CH],
                                      in_=dst_h[k * 128:(k + 1) * 128,
                                                c * CH:(c + 1) * CH])
                    lsp[c % 2].inc(ins, 16)
                    ins = g.dma_start(out=p.fbuf[:, 4096 + k * CH:4096 + (k + 1) * CH],
                                      in_=p.d[srcs[0]][k * 128:(k + 1) * 128,
                                                       c * CH:(c + 1) * CH])
                    lsp[c % 2].inc(ins, 16)
                    if nsrc == 2:
                        ins = g.dma_start(
                            out=p.stf[:, 4096 + k * CH:4096 + (k + 1) * CH],
                            in_=p.d[srcs[1]][k * 128:(k + 1) * 128,
                                             c * CH:(c + 1) * CH])
                        lsp[c % 2].inc(ins, 16)

        @b.vector
        def _(v):
            per = 16 * 8 * (1 + nsrc)
            for c in range(NCH):
                if c >= 1:
                    v.wait_ge(p.ST.h, ST0 + 16 * 8 * c)
                v.wait_ge(lsp[c % 2].h, LSP0[c % 2] + per * (c // 2 + 1))
                for k in range(8):
                    out = p.stf[:, k * CH:(k + 1) * CH]
                    if nsrc == 2:
                        v.tensor_add(out, p.fbuf[:, k * CH:(k + 1) * CH],
                                     p.fbuf[:, 4096 + k * CH:4096 + (k + 1) * CH])
                        ins = v.tensor_add(out, out,
                                           p.stf[:, 4096 + k * CH:4096 + (k + 1) * CH])
                    else:
                        ins = v.tensor_add(out, p.fbuf[:, k * CH:(k + 1) * CH],
                                           p.fbuf[:, 4096 + k * CH:4096 + (k + 1) * CH])
                    p.VS.inc(ins)

        @b.sync
        def _(sy):
            for c in range(NCH):
                sy.wait_ge(p.VS.h, VS0 + 8 * (c + 1))
                for k in range(8):
                    ins = sy.dma_start(out=dst_h[k * 128:(k + 1) * 128,
                                                 c * CH:(c + 1) * CH],
                                       in_=p.stf[:, k * CH:(k + 1) * CH])
                    p.ST.inc(ins, 16)
            sy.wait_ge(p.ST.h, p.ST.n)


def ins_last(v, out):
    raise RuntimeError("unused")


def st_attention(p, H, QT, KT, VT, OUT, kvmap, tiles_fn, scale, biasC=0.0,
                 out_f32=False, sum_floor=1e-30, clamp=None):
    nc = p.nc
    QT_h, KT_h, VT_h, OUT_h = p.d[QT], p.d[KT], p.d[VT], p.d[OUT]
    with ExitStack() as ps_ctx, nc.Block() as b:
        ps_s = [p.psum(ps_ctx), p.psum(ps_ctx)]
        po = p.psum(ps_ctx)
        px = p.psum(ps_ctx)
        pt = p.psum(ps_ctx, (128, 512), BF16)
        MM0, EV0, VS0, SS0, NM0, ST0 = (p.MM.n, p.EV.n, p.VS.n,
                                        p.SS.n, p.NM.n, p.ST.n)
        tiles_all = {c: tiles_fn(c) for c in range(NCH)}
        hcs = [(h, c) for h in range(H) for c in range(NCH)]
        glob = [(h, c, i) for (h, c) in hcs for i in range(len(tiles_all[c]))]
        gidx = {t: n for n, t in enumerate(glob)}
        masked_seq = [t for t in glob if tiles_all[t[1]][t[2]][1] is not None]
        midx = {t: n for n, t in enumerate(masked_seq)}
        s_done, av_done, bc_done = {}, {}, {}
        mm = MM0
        for (h, c) in hcs:
            for i in range(len(tiles_all[c])):
                mm += 1; s_done[(h, c, i)] = mm
                mm += 1; av_done[(h, c, i)] = mm
            mm += 1; bc_done[(h, c)] = mm
        exp_done = {}
        ev = EV0
        for t_ in glob:
            ev += 1; exp_done[t_] = ev
        vtok_done, norm_sdone = {}, {}
        ss = SS0
        for h in range(H):
            for r in range(4):
                ss += 1; vtok_done[(h, r)] = ss
            for c in range(NCH):
                for i in range(len(tiles_all[c])):
                    pass
                ss += 1; norm_sdone[(h, c)] = ss
        mask_done, recip_done, fin_done, clamp_done = {}, {}, {}, {}
        vs = VS0
        for (h, c) in hcs:
            for i, (tk, mk) in enumerate(tiles_all[c]):
                if clamp is not None:
                    vs += 1; clamp_done[(h, c, i)] = vs
                if mk is not None:
                    vs += 1; mask_done[(h, c, i)] = vs
            vs += 1; recip_done[(h, c)] = vs
            vs += 1; fin_done[(h, c)] = vs
        ks_done, q_done = {}, {}
        KV0 = p.KV.n
        QSP0 = [p.QS1.n, p.QS2.n]
        qacc = [QSP0[0], QSP0[1]]
        for h in range(H):
            ks_done[h] = KV0 + 32 * (h + 1)
            for c in range(NCH):
                qacc[c % 2] += 16
                q_done[(h, c)] = qacc[c % 2]
        st_done = {}
        stc = ST0
        for (h, c) in hcs:
            stc += 16; st_done[(h, c)] = stc
        last_tile = {(h, c): (h, c, len(tiles_all[c]) - 1) for (h, c) in hcs}

        AX0 = p.AX.n

        @b.gpsimd
        def _(g):
            ones_view = p.vtok[:, :].rearrange("p (n c) -> p n c", c=65)
            p.AX.inc(g.memset(ones_view[:, :, 64:65], 1.0))
            for h in range(H):
                if h >= 1:
                    g.wait_ge(p.MM.h, bc_done[(h - 1, NCH - 1)])
                kv = kvmap(h)
                p.KV.inc(g.dma_start(out=p.ksb[:, :],
                                     in_=KT_h[64 * kv:64 * kv + 64, :]), 16)
                p.KV.inc(g.dma_start(out=p.vsb[:, :],
                                     in_=VT_h[64 * kv:64 * kv + 64, :]), 16)
                for c in range(NCH):
                    n = h * NCH + c
                    if n >= 2:
                        g.wait_ge(p.MM.h, s_done[last_tile[hcs[n - 2]]])
                    qp = [p.QS1, p.QS2][c % 2]
                    qp.inc(g.dma_start(
                        out=p.qsb[:, (c % 2) * CH:(c % 2 + 1) * CH],
                        in_=QT_h[64 * h:64 * h + 64, c * CH:(c + 1) * CH]), 16)

        @b.tensor
        def _(t):
            for h in range(H):
                t.wait_ge(p.KV.h, ks_done[h])
                for r in range(4):
                    if r >= 1:
                        t.wait_ge(p.SS.h, vtok_done[(h, r - 1)])
                    elif h >= 1:
                        t.wait_ge(p.SS.h, vtok_done[(h - 1, 3)])
                    ins = None
                    for jj in range(8):
                        j = r * 8 + jj
                        ins = t.transpose(pt[:, jj * 64:(jj + 1) * 64],
                                          p.vsb[:, j * 128:(j + 1) * 128],
                                          p.ident[0:64, 0:64])
                    p.NM.inc(ins)
                for c in range(NCH):
                    tl = tiles_all[c]
                    t.wait_ge([p.QS1, p.QS2][c % 2].h, q_done[(h, c)])
                    if c == 0:
                        t.wait_ge(p.SS.h, vtok_done[(h, 3)])
                    n = h * NCH + c
                    if n == 0:
                        t.wait_ge(p.AX.h, AX0 + 1)
                    if n >= 1:
                        t.wait_ge(p.SS.h, norm_sdone[hcs[n - 1]])
                    for i, (tk, mk) in enumerate(tl):
                        gi = gidx[(h, c, i)]
                        if gi >= 2:
                            t.wait_ge(p.EV.h, exp_done[glob[gi - 2]])
                        ins = t.matmul(ps_s[gi % 2][:, :],
                                       p.ksb[:, tk * 128:(tk + 1) * 128],
                                       p.qsb[:, (c % 2) * CH:(c % 2 + 1) * CH],
                                       start=True, stop=True, skip_group_check=True)
                        p.MM.inc(ins)
                        if mk is not None:
                            t.wait_ge(p.VS.h, mask_done[(h, c, i)])
                            mi = midx[(h, c, i)]
                            av_src = p.pmk[:, (mi % 2) * CH:(mi % 2 + 1) * CH]
                        else:
                            t.wait_ge(p.EV.h, exp_done[(h, c, i)])
                            av_src = p.psb[:, (gi % 4) * CH:(gi % 4 + 1) * CH]
                        ins = t.matmul(po[0:65, :],
                                       p.vtok[:, tk * 65:(tk + 1) * 65],
                                       av_src,
                                       start=(i == 0), stop=(i == len(tl) - 1),
                                       skip_group_check=True)
                        p.MM.inc(ins)
                    t.wait_ge(p.VS.h, recip_done[(h, c)])
                    par = n % 2
                    ins = t.matmul(px[0:64, :], p.onesf[0:1, 0:64],
                                   p.small[0:1, par * 512:par * 512 + 512],
                                   start=True, stop=True, skip_group_check=True)
                    p.MM.inc(ins)

        @b.scalar
        def _(s):
            vt_view = p.vtok[:, :].rearrange("p (n c) -> p n c", c=65)
            pt_view = pt[:, :].rearrange("p (n c) -> p n c", c=64)
            for h in range(H):
                for r in range(4):
                    s.wait_ge(p.NM.h, NM0 + h * 4 + r + 1)
                    ins = s.copy(vt_view[:, r * 8:(r + 1) * 8, 0:64], pt_view)
                    p.SS.inc(ins)
                for c in range(NCH):
                    tl = tiles_all[c]
                    n = h * NCH + c
                    for i in range(len(tl)):
                        gi = gidx[(h, c, i)]
                        if clamp is not None:
                            s.wait_ge(p.VS.h, clamp_done[(h, c, i)])
                        else:
                            s.wait_ge(p.MM.h, s_done[(h, c, i)])
                        if gi >= 4:
                            s.wait_ge(p.MM.h, av_done[glob[gi - 4]])
                        ins = s.activation(p.psb[:, (gi % 4) * CH:(gi % 4 + 1) * CH],
                                           ps_s[gi % 2][:, :], AF.Exp,
                                           bias=-biasC, scale=scale)
                        p.EV.inc(ins)
                    s.wait_ge(p.MM.h, av_done[last_tile[(h, c)]])
                    if n >= 2:
                        s.wait_ge(p.VS.h, fin_done[hcs[n - 2]])
                    par = n % 2
                    of32 = p.stf[0:64, par * CH:(par + 1) * CH]
                    s.copy(of32, po[0:64, :])
                    ins = s.add(p.small[0:1, par * 512:par * 512 + 512],
                                po[64:65, :], sum_floor)
                    p.SS.inc(ins)

        @b.vector
        def _(v):
            for h in range(H):
                for c in range(NCH):
                    tl = tiles_all[c]
                    n = h * NCH + c
                    for i, (tk, mk) in enumerate(tl):
                        gi = gidx[(h, c, i)]
                        if clamp is not None:
                            v.wait_ge(p.MM.h, s_done[(h, c, i)])
                            ins = v.tensor_scalar_min(ps_s[gi % 2][:, :],
                                                      ps_s[gi % 2][:, :], clamp)
                            p.VS.inc(ins)
                        if mk is None:
                            continue
                        mi = midx[(h, c, i)]
                        v.wait_ge(p.EV.h, exp_done[(h, c, i)])
                        if mi >= 2:
                            v.wait_ge(p.MM.h, av_done[masked_seq[mi - 2]])
                        ins = v.tensor_mul(
                            p.pmk[:, (mi % 2) * CH:(mi % 2 + 1) * CH],
                            p.psb[:, (gi % 4) * CH:(gi % 4 + 1) * CH],
                            p.maskc[:, mk:mk + CH])
                        p.VS.inc(ins)
                    v.wait_ge(p.SS.h, norm_sdone[(h, c)])
                    par = n % 2
                    ins = v.reciprocal(p.small[0:1, par * 512:par * 512 + 512],
                                       p.small[0:1, par * 512:par * 512 + 512])
                    p.VS.inc(ins)
                    v.wait_ge(p.MM.h, bc_done[(h, c)])
                    if n >= 4:
                        v.wait_ge(p.ST.h, st_done[hcs[n - 1]])
                    of32 = p.stf[0:64, par * CH:(par + 1) * CH]
                    if out_f32:
                        out_slot = p.stf[0:64, (4 + n % 4) * CH:(5 + n % 4) * CH]
                    else:
                        out_slot = p.stbf[0:64, (n % 4) * CH:(n % 4 + 1) * CH]
                    ins = v.tensor_mul(out_slot, of32, px[0:64, :])
                    p.VS.inc(ins)

        @b.sync
        def _(sy):
            for (h, c) in hcs:
                n = h * NCH + c
                sy.wait_ge(p.VS.h, fin_done[(h, c)])
                if out_f32:
                    src_ap = p.stf[0:64, (4 + n % 4) * CH:(5 + n % 4) * CH]
                else:
                    src_ap = p.stbf[0:64, (n % 4) * CH:(n % 4 + 1) * CH]
                ins = sy.dma_start(out=OUT_h[64 * h:64 * h + 64, c * CH:(c + 1) * CH],
                                  in_=src_ap)
                p.ST.inc(ins, 16)
            sy.wait_ge(p.ST.h, p.ST.n)


def tiles_causal(c):
    return [(tk, (tk - 4 * c) * CH if tk >= 4 * c else None)
            for tk in range(4 * (c + 1))]


def tiles_full(c):
    return [(tk, None) for tk in range(32)]


def tiles_window(c, nj, base_col):
    out = []
    for j in range(nj):
        tk = 4 * c - (nj - 4) + j
        if 0 <= tk < 32:
            out.append((tk, base_col + j * CH))
    return out


def st_l0ffn_hg(p, SRC="S1", DST="G0"):
    nc = p.nc
    src_h, dst_h = p.d[SRC], p.d[DST]
    with ExitStack() as ps_ctx, nc.Block() as b:
        psg = p.psum(ps_ctx)
        lsp = [p.LS, p.LSB]
        LSP0 = [p.LS.n, p.LSB.n]
        LA0, SS0, MM0, ST0 = p.LA.n, p.SS.n, p.MM.n, p.ST.n

        @b.gpsimd
        def _(g):
            p.LA.inc(g.dma_start(out=p.small[0:64, 0:16], in_=p.d["l0f_w1"][:, :]), 16)
            p.LA.inc(g.dma_start(out=p.small[0:64, 16:32], in_=p.d["l0f_b1"][:, :]), 16)
            p.LA.inc(g.dma_start(out=p.small[0:64, 32:48], in_=p.d["l0f_w2"][:, :]), 16)
            p.LA.inc(g.dma_start(out=p.small[:, 48:49], in_=p.d["l0f_b2"][:, :]), 16)
            for c in range(NCH):
                if c >= 1:
                    g.wait_ge(p.SS.h, SS0 + 2 * c)
                ins = g.dma_start(out=p.abuf[0:64, 4096:4096 + CH],
                                  in_=src_h[0:64, c * CH:(c + 1) * CH])
                lsp[c % 2].inc(ins, 16)

        @b.scalar
        def _(s):
            s.wait_ge(p.LA.h, LA0 + 64)
            ins = None
            for t_ in range(8):
                s.mul(p.abuf[0:64, 12288 + t_ * 64:12288 + (t_ + 1) * 64],
                      p.ident[0:64, 0:64], p.small[0:64, 32 + 2 * t_:33 + 2 * t_])
                ins = s.mul(p.abuf[64:128, 12288 + t_ * 64:12288 + (t_ + 1) * 64],
                            p.ident[0:64, 0:64], p.small[0:64, 33 + 2 * t_:34 + 2 * t_])
            p.SS.inc(ins)
            for c in range(NCH):
                s.wait_ge(lsp[c % 2].h, LSP0[c % 2] + 16 * (c // 2 + 1))
                if c >= 1:
                    s.wait_ge(p.MM.h, MM0 + c)
                ins = None
                for t_ in range(8):
                    i0, i1 = 2 * t_, 2 * t_ + 1
                    s.activation(p.abuf[0:64, t_ * CH:(t_ + 1) * CH],
                                 p.abuf[0:64, 4096:4096 + CH], AF.Relu,
                                 bias=p.small[0:64, 16 + i0:17 + i0],
                                 scale=p.small[0:64, 0 + i0:1 + i0])
                    ins = s.activation(p.abuf[64:128, t_ * CH:(t_ + 1) * CH],
                                       p.abuf[0:64, 4096:4096 + CH], AF.Relu,
                                       bias=p.small[0:64, 16 + i1:17 + i1],
                                       scale=p.small[0:64, 0 + i1:1 + i1])
                p.SS.inc(ins)
                s.wait_ge(p.MM.h, MM0 + c + 1)
                if c >= 2:
                    s.wait_ge(p.ST.h, ST0 + 16 * c)
                ins = s.add(p.stbf[0:64, (c % 2) * CH:(c % 2 + 1) * CH],
                            psg[0:64, :], p.small[0:64, 48:49])
                p.SS.inc(ins)

        @b.tensor
        def _(t):
            for c in range(NCH):
                t.wait_ge(p.SS.h, SS0 + 2 * c + 2)
                ins = None
                for t_ in range(8):
                    ins = t.matmul(psg[0:64, :],
                                   p.abuf[:, 12288 + t_ * 64:12288 + (t_ + 1) * 64],
                                   p.abuf[:, t_ * CH:(t_ + 1) * CH],
                                   start=(t_ == 0), stop=(t_ == 7),
                                   skip_group_check=True)
                p.MM.inc(ins)

        @b.sync
        def _(sy):
            for c in range(NCH):
                sy.wait_ge(p.SS.h, SS0 + 2 * c + 3)
                ins = sy.dma_start(out=dst_h[0:64, c * CH:(c + 1) * CH],
                                   in_=p.stbf[0:64, (c % 2) * CH:(c % 2 + 1) * CH])
                p.ST.inc(ins, 16)
            sy.wait_ge(p.ST.h, p.ST.n)



SCRATCH = [
    ("XN", (DM, T), BF16), ("XNF", (DM, T), F32),
    ("Q1", (64, T), BF16), ("KV1", (32, T), BF16),
    ("QT", (DM, T), BF16), ("KT", (DM, T), BF16), ("VT", (DM, T), BF16),
    ("QT2", (DM, T), BF16), ("KT2", (DM, T), BF16),
    ("OT", (DM, T), BF16), ("AT", (DM, T), F32), ("FT", (DM, T), F32),
    ("S1", (64, T), BF16), ("G0", (64, T), BF16),
    ("CAT", (DM, T), BF16),
    ("FH", (2752, T), BF16), ("EX", (2048, T), BF16),
    ("RW", (16, T), F32), ("SB1", (DM, T), BF16),
    ("KM", (DM, DM), BF16), ("QM", (DM, DM), BF16), ("VM", (DM, DM), BF16),
    ("PRS", (16, 16), F32), ("DBG", (64, 1024), F32),
]


def build_L0(p, upto=99):
    st_layernorm(p, "XOUT", ("XN", 0))
    if upto < 1:
        return
    st_linear(p, ("XN", 0), DM, ("l0_wqa", 0, 0), 64, ("Q1", 0))
    if upto < 2:
        return
    st_linear(p, ("Q1", 0), 64, ("l0_wqb", 0, 0), DM, ("QT", 0),
              evac="rope", rope=("c_l0cos", "c_l0sin"))
    if upto < 3:
        return
    st_linear(p, ("XN", 0), DM, ("l0_wkva", 0, 0), 32, ("KV1", 0))
    st_linear(p, ("KV1", 0), 32, ("l0_wkb", 0, 0), 256, ("KT", 0),
              evac="rope", rope=("c_l0cos", "c_l0sin"))
    st_linear(p, ("KV1", 0), 32, ("l0_wvb", 0, 0), 256, ("VT", 0))
    if upto < 4:
        return
    st_attention(p, 16, "QT", "KT", "VT", "OT", kvmap=lambda h: h // 4,
                 tiles_fn=tiles_causal, scale=0.125)
    if upto < 5:
        return
    st_linear(p, ("OT", 0), DM, ("l0_wout", 0, 0), DM, ("AT", 0),
              dst_dtype=F32, bias="l0_bout")
    if upto < 6:
        return
    st_linear(p, ("XN", 0), DM, ("l0f_win", 0, 0), 64, ("S1", 0), bias="l0f_bin")
    st_l0ffn_hg(p)
    st_linear(p, ("G0", 0), 64, ("l0f_wout", 0, 0), DM, ("FT", 0),
              dst_dtype=F32, bias="l0f_bout")
    if upto < 7:
        return
    st_add(p, "XOUT", ["AT", "FT"])


def build_program(host, parts=("L0", "L1", "L2", "L3"), dbg=()):
    nc = bass.Bass("TRN2", target_bir_lowering=False)
    with ExitStack() as ctx:
        p = P(nc, ctx)
        p.dram("XT0", (DM, T), F32, kind="ExternalInput")
        p.dram("XOUT", (DM, T), F32, kind="ExternalOutput")
        for name, arr in host.items():
            if arr.dtype == BFD:
                dt = BF16
            elif arr.dtype == np.float16:
                dt = F16
            else:
                dt = F32
            p.dram(name, arr.shape, dt, kind="ExternalInput")
        for name, shape, dt in SCRATCH:
            p.dram(name, shape, dt,
                   kind="ExternalOutput" if name in dbg else "Internal")
        st_init(p)
        for part in parts:
            if part.startswith("L0"):
                build_L0(p, upto=int(part[2:]) if len(part) > 2 else 99)
        for part in parts:
            if part.startswith("L1"):
                build_L1(p, upto=int(part[2:]) if len(part) > 2 else 99)
        for part in parts:
            if part.startswith("L2"):
                build_L2(p, upto=int(part[2:]) if len(part) > 2 else 99)
        for part in parts:
            if part.startswith("L3") and not part.startswith("L3F"):
                build_L3(p, upto=int(part[2:]) if len(part) > 2 else 99)
    return nc


def kernel(**inputs):
    consts = host_constants()
    w = prep_weights(inputs)
    host = {**consts, **w}
    nc = build_program(host)
    x = inputs["L0_attn_x"]
    in_maps = []
    for core in range(8):
        m = dict(host)
        m["XT0"] = np.ascontiguousarray(x[core].T)
        in_maps.append(m)
    res = run_bass_kernel_spmd(nc, in_maps, core_ids=list(range(8)))
    return np.stack([np.ascontiguousarray(r["XOUT"].T) for r in res.results])


def st_shift_bf16(p, src, dst):
    nc = p.nc
    src_h, dst_h = p.d[src], p.d[dst]
    with nc.Block() as b:
        lsp = [p.LS, p.LSB]
        LSP0 = [p.LS.n, p.LSB.n]
        SS0, ST0 = p.SS.n, p.ST.n

        @b.gpsimd
        def _(g):
            for c in range(NCH):
                if c >= 1:
                    g.wait_ge(p.SS.h, SS0 + c)
                for k in range(8):
                    if k >= 4:
                        if c == 0:
                            g.memset(p.fbuf[:, k * CH:k * CH + 1], 0.0)
                            ins = g.dma_start(
                                out=p.fbuf[:, k * CH + 1:(k + 1) * CH],
                                in_=src_h[k * 128:(k + 1) * 128, 0:CH - 1])
                        else:
                            ins = g.dma_start(
                                out=p.fbuf[:, k * CH:(k + 1) * CH],
                                in_=src_h[k * 128:(k + 1) * 128,
                                          c * CH - 1:(c + 1) * CH - 1])
                    else:
                        ins = g.dma_start(
                            out=p.fbuf[:, k * CH:(k + 1) * CH],
                            in_=src_h[k * 128:(k + 1) * 128, c * CH:(c + 1) * CH])
                    lsp[c % 2].inc(ins, 16)

        @b.scalar
        def _(s):
            for c in range(NCH):
                s.wait_ge(lsp[c % 2].h, LSP0[c % 2] + 16 * 8 * (c // 2 + 1))
                if c >= 2:
                    s.wait_ge(p.ST.h, ST0 + 16 * 8 * c)
                ins = None
                for k in range(8):
                    ins = s.copy(p.stbf[:, ((c % 2) * 8 + k) * CH:
                                        ((c % 2) * 8 + k + 1) * CH],
                                 p.fbuf[:, k * CH:(k + 1) * CH])
                p.SS.inc(ins)

        @b.sync
        def _(sy):
            for c in range(NCH):
                sy.wait_ge(p.SS.h, SS0 + c + 1)
                for k in range(8):
                    ins = sy.dma_start(
                        out=dst_h[k * 128:(k + 1) * 128, c * CH:(c + 1) * CH],
                        in_=p.stbf[:, ((c % 2) * 8 + k) * CH:((c % 2) * 8 + k + 1) * CH])
                    p.ST.inc(ins, 16)
            sy.wait_ge(p.ST.h, p.ST.n)


def st_xpos(p, src, dst, cosname, sinname, sccol, scscol, rows):
    nc = p.nc
    src_h, dst_h = p.d[src], p.d[dst]
    kt = rows // 128
    with ExitStack() as ps_ctx, nc.Block() as b:
        psN = p.psum(ps_ctx, (2, 512))
        psB = p.psum(ps_ctx)
        lsp = [p.LS, p.LSB]
        LSP0 = [p.LS.n, p.LSB.n]
        LA0, SS0, MM0, VS0, ST0 = p.LA.n, p.SS.n, p.MM.n, p.VS.n, p.ST.n
        n_t = NCH * kt

        @b.gpsimd
        def _(g):
            p.LA.inc(g.dma_start(out=p.fbuf[:, 0:4096], in_=p.d[cosname][:, :]), 16)
            p.LA.inc(g.dma_start(out=p.fbuf[:, 4096:8192], in_=p.d[sinname][:, :]), 16)
            p.LA.inc(g.dma_start(out=p.small[:, 0:1], in_=p.d[sccol][:, :]), 16)
            p.LA.inc(g.dma_start(out=p.small[:, 1:2], in_=p.d[scscol][:, :]), 16)
            for c in range(NCH):
                if c >= 1:
                    g.wait_ge(p.SS.h, SS0 + 2 * kt * c)
                    g.wait_ge(p.VS.h, VS0 + 3 * kt * c - 1)
                for k in range(kt):
                    ins = g.dma_start(
                        out=p.abuf[:, k * CH:(k + 1) * CH],
                        in_=src_h[k * 128:(k + 1) * 128, c * CH:(c + 1) * CH])
                    lsp[c % 2].inc(ins, 16)

        @b.scalar
        def _(s):
            for c in range(NCH):
                s.wait_ge(lsp[c % 2].h, LSP0[c % 2] + 16 * kt * (c // 2 + 1))
                for k in range(kt):
                    t = c * kt + k
                    ins = s.activation(p.abuf[:, 4096 + k * CH:4096 + (k + 1) * CH],
                                       p.abuf[:, k * CH:(k + 1) * CH], AF.Square)
                    p.SS.inc(ins)
                    s.wait_ge(p.MM.h, MM0 + 2 * t + 1)
                    ins = s.activation(p.small[0:2, 512:1024], psN[:, :], AF.Sqrt)
                    p.SS.inc(ins)

        @b.tensor
        def _(t_):
            for c in range(NCH):
                for k in range(kt):
                    t = c * kt + k
                    t_.wait_ge(p.SS.h, SS0 + 2 * t + 1)
                    ins = t_.matmul(psN[:, :], p.oblk[:, :],
                                    p.abuf[:, 4096 + k * CH:4096 + (k + 1) * CH],
                                    start=True, stop=True, skip_group_check=True)
                    p.MM.inc(ins)
                    t_.wait_ge(p.VS.h, VS0 + 3 * t + 1)
                    ins = t_.matmul(psB[:, :], p.oblkT[:, :],
                                    p.small[0:2, 512:1024],
                                    start=True, stop=True, skip_group_check=True)
                    p.MM.inc(ins)

        @b.vector
        def _(v):
            for c in range(NCH):
                v.wait_ge(p.LA.h, LA0 + 64)
                v.tensor_scalar(p.fbuf[:, 8192:8704],
                                p.fbuf[:, c * CH:(c + 1) * CH],
                                p.small[:, 0:1], None, ALU.mult)
                v.tensor_scalar(p.fbuf[:, 8704:9216],
                                p.fbuf[:, 4096 + c * CH:4096 + (c + 1) * CH],
                                p.small[:, 1:2], None, ALU.mult)
                for k in range(kt):
                    t = c * kt + k
                    v.wait_ge(p.SS.h, SS0 + 2 * t + 2)
                    v.tensor_scalar_max(p.small[0:2, 512:1024],
                                        p.small[0:2, 512:1024], 1e-12)
                    ins = v.reciprocal(p.small[0:2, 512:1024],
                                       p.small[0:2, 512:1024])
                    p.VS.inc(ins)
                    v.wait_ge(p.MM.h, MM0 + 2 * t + 2)
                    qn = p.stf[:, 0:CH]
                    qnsw = p.stf[:, 3 * CH:4 * CH]
                    t1 = p.stf[:, CH:2 * CH]
                    t2 = p.stf[:, 2 * CH:3 * CH]
                    qt = p.abuf[:, k * CH:(k + 1) * CH]
                    v.tensor_mul(qn, qt, psB[:, :])
                    v.tensor_mul(qnsw[0:32, :], qt[32:64, :], psB[32:64, :])
                    v.tensor_mul(qnsw[32:64, :], qt[0:32, :], psB[0:32, :])
                    v.tensor_mul(qnsw[64:96, :], qt[96:128, :], psB[96:128, :])
                    ins = v.tensor_mul(qnsw[96:128, :], qt[64:96, :], psB[64:96, :])
                    p.VS.inc(ins)
                    v.tensor_mul(t1, qn, p.fbuf[:, 8192:8704])
                    v.tensor_mul(t2, qnsw, p.fbuf[:, 8704:9216])
                    if t >= 4:
                        v.wait_ge(p.ST.h, ST0 + 16 * t)
                    ins = v.tensor_add(p.stbf[:, (t % 4) * CH:(t % 4 + 1) * CH],
                                       t1, t2)
                    p.VS.inc(ins)

        @b.sync
        def _(sy):
            for c in range(NCH):
                for k in range(kt):
                    t = c * kt + k
                    sy.wait_ge(p.VS.h, VS0 + 3 * t + 3)
                    ins = sy.dma_start(
                        out=dst_h[k * 128:(k + 1) * 128, c * CH:(c + 1) * CH],
                        in_=p.stbf[:, (t % 4) * CH:(t % 4 + 1) * CH])
                    p.ST.inc(ins, 16)
            sy.wait_ge(p.ST.h, p.ST.n)


def st_diffcat(p):
    nc = p.nc
    with ExitStack() as ps_ctx, nc.Block() as b:
        psL = p.psum(ps_ctx, (1, 512))
        psB = p.psum(ps_ctx, (128, 512))
        lsp = [p.LS, p.LSB]
        LSP0 = [p.LS.n, p.LSB.n]
        LA0, SS0, MM0, VS0, ST0 = p.LA.n, p.SS.n, p.MM.n, p.VS.n, p.ST.n

        @b.gpsimd
        def _(g):
            for i, nm in enumerate(["l1_lambda_q1", "l1_lambda_k1",
                                    "l1_lambda_q2", "l1_lambda_k2"]):
                p.LA.inc(g.dma_start(out=p.small[0:8, 64 * i:64 * i + 64],
                                     in_=p.d[nm][:, :]), 16)
            p.LA.inc(g.dma_start(out=p.d["CAT"][512:1024, :],
                                 in_=p.d["OT"][512:1024, :]), 16)
            for c in range(NCH):
                if c >= 1:
                    g.wait_ge(p.VS.h, VS0 + 4 + 4 * c)
                for k in range(4):
                    ins = g.dma_start(out=p.abuf[:, k * CH:(k + 1) * CH],
                                      in_=p.d["OT"][k * 128:(k + 1) * 128,
                                                    c * CH:(c + 1) * CH])
                    lsp[c % 2].inc(ins, 16)
                    ins = g.dma_start(out=p.abuf[:, 4096 + k * CH:4096 + (k + 1) * CH],
                                      in_=p.d["OT"][512 + k * 128:512 + (k + 1) * 128,
                                                    c * CH:(c + 1) * CH])
                    lsp[c % 2].inc(ins, 16)

        @b.vector
        def _(v):
            v.wait_ge(p.LA.h, LA0 + 64)
            v.tensor_mul(p.small[0:8, 256:320], p.small[0:8, 0:64],
                         p.small[0:8, 64:128])
            v.tensor_mul(p.small[0:8, 320:384], p.small[0:8, 128:192],
                         p.small[0:8, 192:256])
            v.reduce_sum(p.small[0:8, 384:385], p.small[0:8, 256:320], axis=AX.X)
            ins = v.reduce_sum(p.small[0:8, 385:386], p.small[0:8, 320:384],
                               axis=AX.X)
            p.VS.inc(ins)
            v.wait_ge(p.SS.h, SS0 + 1)
            ins = v.tensor_scalar_min(p.small[0:8, 390:392],
                                      p.small[0:8, 390:392], 5.0)
            p.VS.inc(ins)
            v.wait_ge(p.SS.h, SS0 + 2)
            v.tensor_sub(p.small[0:8, 388:389], p.small[0:8, 387:388],
                         p.small[0:8, 386:387])
            ins = v.tensor_copy(p.smallb[0:8, 0:1], p.small[0:8, 388:389])
            p.VS.inc(ins)
            v.wait_ge(p.MM.h, MM0 + 1)
            ins = v.tensor_scalar(p.small[0:1, 400:401], psL[0:1, 0:1], 0.125,
                                  -0.8, ALU.mult, ALU.add)
            p.VS.inc(ins)
            for c in range(NCH):
                v.wait_ge(p.MM.h, MM0 + 2)
                v.wait_ge(lsp[c % 2].h, LSP0[c % 2] + 16 * 8 * (c // 2 + 1))
                if c >= 1:
                    v.wait_ge(p.ST.h, ST0 + 16 * 4 * c)
                for k in range(4):
                    ins = v.scalar_tensor_tensor(
                        p.stbf[0:128, k * CH:(k + 1) * CH],
                        p.abuf[:, 4096 + k * CH:4096 + (k + 1) * CH],
                        psB[:, 0:1],
                        p.abuf[:, k * CH:(k + 1) * CH],
                        ALU.mult, ALU.add)
                    p.VS.inc(ins)

        @b.scalar
        def _(s):
            s.wait_ge(p.VS.h, VS0 + 1)
            ins = s.copy(p.small[0:8, 390:392], p.small[0:8, 384:386])
            p.SS.inc(ins)
            s.wait_ge(p.VS.h, VS0 + 2)
            ins = s.activation(p.small[0:8, 386:388], p.small[0:8, 390:392], AF.Exp)
            p.SS.inc(ins)

        @b.tensor
        def _(t):
            t.wait_ge(p.VS.h, VS0 + 3)
            ins = t.matmul(psL[0:1, 0:1], p.onesb[0:8, :], p.smallb[0:8, 0:1],
                           start=True, stop=True, skip_group_check=True)
            p.MM.inc(ins)
            t.wait_ge(p.VS.h, VS0 + 4)
            ins = t.matmul(psB[:, 0:1], p.onesf[0:1, :], p.small[0:1, 400:401],
                           start=True, stop=True, skip_group_check=True)
            p.MM.inc(ins)

        @b.sync
        def _(sy):
            sy.wait_ge(p.LA.h, p.LA.n)
            for c in range(NCH):
                for k in range(4):
                    sy.wait_ge(p.VS.h, VS0 + 4 + 4 * c + k + 1)
                    ins = sy.dma_start(
                        out=p.d["CAT"][k * 128:(k + 1) * 128, c * CH:(c + 1) * CH],
                        in_=p.stbf[0:128, k * CH:(k + 1) * CH])
                    p.ST.inc(ins, 16)
            sy.wait_ge(p.ST.h, p.ST.n)


def build_L1(p, upto=99):
    st_layernorm(p, "XOUT", ("XN", 0), dst_f32=("XNF", 0))
    if upto < 1:
        return
    st_linear(p, ("XN", 0), DM, ("l1_wqt", 0, 0), DM, ("QT", 0), bias="l1_bq")
    st_linear(p, ("XN", 0), DM, ("l1_wkt", 0, 0), DM, ("KT", 0), bias="l1_bk")
    st_linear(p, ("XN", 0), DM, ("l1_wvt", 0, 0), DM, ("VT", 0), bias="l1_bv")
    if upto < 2:
        return
    st_attention(p, 16, "QT", "KT", "VT", "OT", kvmap=lambda h: h,
                 tiles_fn=tiles_full, scale=0.125)
    if upto < 3:
        return
    st_diffcat(p)
    st_linear(p, ("CAT", 0), DM, ("l1_wpt", 0, 0), DM, ("AT", 0),
              dst_dtype=F32, bias="l1_bp")
    st_add(p, "XOUT", ["AT"])
    if upto < 4:
        return
    for i in range(6):
        if upto < 4 + 10 * i + 1:
            return
        st_shift_bf16(p, "XNF", "SB1")
        st_linear(p, ("SB1", 0), DM, (f"l1f{i}_wq", 0, 0), 1024, ("QT", 0))
        st_linear(p, ("SB1", 0), DM, (f"l1f{i}_wk", 0, 0), 1024, ("KT", 0))
        st_linear(p, ("SB1", 0), DM, (f"l1f{i}_wv", 0, 0), 1024, ("VT", 0))
        if upto < 4 + 10 * i + 2:
            return
        st_xpos(p, "QT", "QT2", "c_xq256c", "c_xq256s",
                f"l1f{i}_qsc", f"l1f{i}_qscs", 1024)
        st_xpos(p, "KT", "KT2", "c_xk256c", "c_xk256s",
                f"l1f{i}_ksc", f"l1f{i}_kscs", 1024)
        if upto < 4 + 10 * i + 3:
            return
        st_attention(p, 16, "QT2", "KT2", "VT", "OT", kvmap=lambda h: h,
                     tiles_fn=lambda c: tiles_window(c, 6, 2048),
                     scale=8.0, biasC=20.0, clamp=8.0)
        if upto < 4 + 10 * i + 4:
            return
        st_linear(p, ("OT", 0), 1024, (f"l1f{i}_wo", 0, 0), DM, ("AT", 0),
                  dst_dtype=F32)
        st_add(p, "XNF", ["AT"])
        if upto < 4 + 10 * i + 5:
            return
        st_layernorm(p, "XNF", ("SB1", 0), gain=f"l1f{i}_lng",
                     beta=f"l1f{i}_lnb", shifted=True)
        if upto < 4 + 10 * i + 6:
            return
        st_linear(p, ("SB1", 0), DM, (f"l1f{i}_w1p", 0, 0), 5504, ("FH", 0),
                  evac="glu", bias=f"l1f{i}_b1p")
        if upto < 4 + 10 * i + 7:
            return
        st_linear(p, ("FH", 0), 2752, (f"l1f{i}_w2p", 0, 0), DM, ("AT", 0),
                  dst_dtype=F32, bias=f"l1f{i}_b2")
        st_add(p, "XNF", ["AT"])
    if upto < 99:
        return
    st_add(p, "XOUT", ["XNF"])


def st_l2mats(p):
    nc = p.nc
    with ExitStack() as ps_ctx:
        psB = p.psum(ps_ctx, (64, 256))
        GP0 = p.GP.n
        with nc.Block() as b0:
            @b0.gpsimd
            def _(g):
                p.GP.inc(g.dma_start(out=p.small[0:16, 0:16],
                                     in_=p.d["l2_wkp"][:, :]), 16)
                p.GP.inc(g.dma_start(out=p.small[0:16, 16:32],
                                     in_=p.d["l2_wqp"][:, :]), 16)
                p.GP.inc(g.dma_start(out=p.small[0:16, 32:48],
                                     in_=p.d["l2_wvp"][:, :]), 16)
                p.GP.inc(g.dma_start(out=p.fbuf[0:64, 0:1024],
                                     in_=p.d["l2_wpredr"][:, :]), 16)
                g.wait_ge(p.GP.h, p.GP.n)
        for mi, (nm, off, use_pred) in enumerate(
                [("KM", 0, False), ("KM", 0, False), ("QM", 16, True),
                 ("QM", 16, True), ("VM", 32, False), ("VM", 32, False)]):
            with nc.Block() as b2:
                @b2.scalar
                def _(s, off=off):
                    s.activation(p.small[0:16, 50:66], p.small[0:16, off:off + 16],
                                 AF.Exp)
            with nc.Block() as b3:
                @b3.vector
                def _(v):
                    v.reduce_sum(p.small[0:16, 49:50], p.small[0:16, 50:66],
                                 axis=AX.X)
            with nc.Block() as b4:
                @b4.vector
                def _(v):
                    v.reciprocal(p.small[0:16, 47:48], p.small[0:16, 49:50])
                    v.tensor_scalar(p.small[0:16, 100:116], p.small[0:16, 50:66],
                                    p.small[0:16, 47:48], None, ALU.mult)
            with nc.Block() as b5:
                @b5.gpsimd
                def _(g):
                    p.GP.inc(g.dma_start(
                        out=p.d["PRS"][:, :].rearrange("a bb -> bb a"),
                        in_=p.small[0:16, 100:116],
                        allow_slow_non_contiguous=True), 16)
                    g.wait_ge(p.GP.h, p.GP.n)
                    p.GP.inc(g.dma_start(
                        out=p.small[0:1, 200:456],
                        in_=p.d["PRS"][:, :].rearrange("a bb -> (a bb)")[None, :]),
                        16)
                    g.wait_ge(p.GP.h, p.GP.n)
            with nc.Block() as b6:
                @b6.tensor
                def _(t):
                    t.matmul(psB[:, :], p.onesf[0:1, 0:64], p.small[0:1, 200:456],
                             start=True, stop=True, skip_group_check=True)
            with nc.Block() as b7:
                @b7.vector
                def _(v):
                    v.tensor_copy(p.small[0:64, 1024:1280], psB[:, :])
            with nc.Block() as b8:
                @b8.vector
                def _(v, use_pred=use_pred):
                    for vv in range(16):
                        for i in range(16):
                            if use_pred:
                                src_ap = p.fbuf[0:64, 64 * i:64 * i + 64]
                            else:
                                src_ap = p.ident[0:64, 0:64]
                            v.tensor_scalar(
                                p.stbf[64 * (vv % 2):64 * (vv % 2) + 64,
                                       (vv // 2) * 1024 + 64 * i:
                                       (vv // 2) * 1024 + 64 * i + 64],
                                src_ap,
                                p.small[0:64, 1024 + 16 * vv + i:
                                        1025 + 16 * vv + i], None, ALU.mult)
            with nc.Block() as b9:
                @b9.sync
                def _(sy, nm=nm):
                    for kt_ in range(8):
                        ins = sy.dma_start(
                            out=p.d[nm][128 * kt_:128 * (kt_ + 1), :],
                            in_=p.stbf[:, kt_ * 1024:(kt_ + 1) * 1024])
                        p.ST.inc(ins, 16)
                    sy.wait_ge(p.ST.h, p.ST.n)


def st_router(p):
    nc = p.nc
    with ExitStack() as ps_ctx:
        psL = p.psum(ps_ctx, (128, 32))
        psW = p.psum(ps_ctx, (32, 128))
        sc = p.small

        def vblk(fn):
            with nc.Block() as bb:
                bb.vector(fn)

        def sblk(fn):
            with nc.Block() as bb:
                bb.scalar(fn)

        def tblk(fn):
            with nc.Block() as bb:
                bb.tensor(fn)

        with nc.Block() as b0:
            @b0.gpsimd
            def _(g):
                p.GP.inc(g.dma_start(out=p.small[:, 0:64],
                                     in_=p.d["l2_wrp"][:, :]), 16)
                p.GP.inc(g.dma_start(out=p.fbuf[:, 4096:4224],
                                     in_=p.d["c_idf"][:, :]), 16)
                g.wait_ge(p.GP.h, p.GP.n)
        for grp in [0] + list(range(8)):
            with nc.Block() as b1:
                @b1.gpsimd
                def _(g, grp=grp):
                    for j in range(4):
                        tt = 4 * grp + j
                        for k in range(8):
                            p.GP.inc(g.dma_start(
                                out=p.fbuf[:, j * 1024 + k * 128:
                                           j * 1024 + (k + 1) * 128],
                                in_=p.d["XNF"][k * 128:(k + 1) * 128,
                                               tt * 128:(tt + 1) * 128]), 16)
                    g.wait_ge(p.GP.h, p.GP.n)
            for _rep in range(2):
                tblk(lambda t: [t.matmul(psL[:, 8 * j:8 * (j + 1)],
                                         p.fbuf[:, j * 1024 + k * 128:
                                                j * 1024 + (k + 1) * 128],
                                         p.small[:, k * 8:(k + 1) * 8],
                                         start=(k == 0), stop=(k == 7),
                                         skip_group_check=True)
                                for j in range(4) for k in range(8)])
                sblk(lambda s: s.activation(sc[:, 96:128], psL[:, :], AF.Exp))
                for j in range(4):
                    vblk(lambda v, j=j: v.reduce_sum(
                        sc[:, 68 + j:69 + j], sc[:, 96 + 8 * j:104 + 8 * j], axis=AX.X))
                vblk(lambda v: v.reciprocal(sc[:, 72:76], sc[:, 68:72]))
                vblk(lambda v: [v.tensor_scalar(sc[:, 128 + 8 * j:136 + 8 * j],
                                                sc[:, 96 + 8 * j:104 + 8 * j],
                                                sc[:, 72 + j:73 + j], None, ALU.mult)
                                for j in range(4)])
                for j in range(4):
                    vblk(lambda v, j=j: v.reduce_max(
                        sc[:, 76 + j:77 + j], sc[:, 128 + 8 * j:136 + 8 * j],
                        axis=AX.X))
                vblk(lambda v: [v.tensor_scalar(sc[:, 160 + 8 * j:168 + 8 * j],
                                                sc[:, 128 + 8 * j:136 + 8 * j],
                                                sc[:, 76 + j:77 + j], None, ALU.is_ge)
                                for j in range(4)])
                vblk(lambda v: v.scalar_tensor_tensor(sc[:, 192:224], sc[:, 160:192],
                                                      -1e9, sc[:, 128:160],
                                                      ALU.mult, ALU.add))
                for j in range(4):
                    vblk(lambda v, j=j: v.reduce_max(
                        sc[:, 80 + j:81 + j], sc[:, 192 + 8 * j:200 + 8 * j],
                        axis=AX.X))
                vblk(lambda v: v.tensor_sub(sc[:, 420:424], sc[:, 76:80], sc[:, 80:84]))
                sblk(lambda s: s.activation(sc[:, 88:92], sc[:, 420:424], AF.Sigmoid))
                vblk(lambda v: [v.tensor_scalar(sc[:, 92:96], sc[:, 88:92], -1.0, 1.0,
                                                ALU.mult, ALU.add)] +
                     [v.tensor_scalar(sc[:, 224 + 8 * j:232 + 8 * j],
                                      sc[:, 192 + 8 * j:200 + 8 * j],
                                      sc[:, 80 + j:81 + j], None, ALU.is_ge)
                      for j in range(4)] +
                     [v.tensor_scalar(sc[:, 256 + 8 * j:264 + 8 * j],
                                      sc[:, 160 + 8 * j:168 + 8 * j],
                                      sc[:, 88 + j:89 + j], None, ALU.mult)
                      for j in range(4)])
                vblk(lambda v: [v.scalar_tensor_tensor(sc[:, 256 + 8 * j:264 + 8 * j],
                                                       sc[:, 224 + 8 * j:232 + 8 * j],
                                                       sc[:, 92 + j:93 + j],
                                                       sc[:, 256 + 8 * j:264 + 8 * j],
                                                       ALU.mult, ALU.add)
                                for j in range(4)])
                tblk(lambda t: t.transpose(psW[:, :], sc[:, 256:288],
                                           p.fbuf[:, 4096:4224]))
                vblk(lambda v: v.tensor_copy(sc[0:32, 288:416], psW[:, :]))
            with nc.Block() as b13:
                @b13.sync
                def _(sy, grp=grp):
                    for j in range(4):
                        tt = 4 * grp + j
                        ins = sy.dma_start(
                            out=p.d["RW"][0:8, tt * 128:(tt + 1) * 128],
                            in_=sc[8 * j:8 * j + 8, 288:416])
                        p.ST.inc(ins, 16)
                    sy.wait_ge(p.ST.h, p.ST.n)


def build_L2(p, upto=99):
    st_layernorm(p, "XOUT", ("XN", 0), dst_f32=("XNF", 0))
    if upto < 1:
        return
    st_l2mats(p)
    if upto < 2:
        return
    st_linear(p, ("XN", 0), DM, ("QM", 0, 0), DM, ("QT", 0))
    st_linear(p, ("XN", 0), DM, ("KM", 0, 0), DM, ("KT", 0))
    st_linear(p, ("XN", 0), DM, ("VM", 0, 0), DM, ("VT", 0))
    if upto < 3:
        return
    st_attention(p, 16, "QT", "KT", "VT", "AT", kvmap=lambda h: h,
                 tiles_fn=tiles_causal, scale=0.125, out_f32=True)
    if upto < 4:
        return
    st_router(p)
    if upto < 5:
        return
    for e in range(8):
        st_linear(p, ("XN", 0), DM, (f"l2_wg{e}", 0, 0), 2048, ("EX", 0),
                  act=AF.Silu)
        st_linear(p, ("XN", 0), DM, (f"l2_wu{e}", 0, 0), 2048, ("EX", 0),
                  evac="mulwith", mul_other=("EX", 0))
        st_linear(p, ("EX", 0), 2048, (f"l2_wd{e}", 0, 0), DM, ("FT", 0),
                  evac="wdown", wrow=("RW", e), waccum=(e > 0))
    if upto < 6:
        return
    st_add(p, "XOUT", ["AT", "FT"])


def build_L3(p, upto=99):
    st_layernorm(p, "XOUT", ("XN", 0), dst_f32=("XNF", 0))
    if upto < 1:
        return
    st_linear(p, ("XN", 0), DM, ("l3_wqt", 0, 0), DM, ("QT", 0))
    st_linear(p, ("XN", 0), DM, ("l3_wkt", 0, 0), DM, ("KT", 0))
    st_linear(p, ("XN", 0), DM, ("l3_wvt", 0, 0), DM, ("VT", 0))
    if upto < 2:
        return
    st_attention(p, 16, "QT", "KT", "VT", "OT", kvmap=lambda h: h,
                 tiles_fn=lambda c: tiles_window(c, 6, 2048), scale=0.125)
    if upto < 3:
        return
    st_linear(p, ("OT", 0), DM, ("l3_wot", 0, 0), DM, ("AT", 0), dst_dtype=F32)
    st_add(p, "XOUT", ["AT"])
    if upto < 4:
        return
    for i in range(2):
        if upto < 4 + 10 * i + 1:
            return
        st_shift_bf16(p, "XNF", "SB1")
        st_linear(p, ("SB1", 0), DM, (f"l3f{i}_wq", 0, 0), 512, ("QT", 0))
        st_linear(p, ("SB1", 0), DM, (f"l3f{i}_wk", 0, 0), 512, ("KT", 0))
        st_linear(p, ("SB1", 0), DM, (f"l3f{i}_wv", 0, 0), 512, ("VT", 0))
        if upto < 4 + 10 * i + 2:
            return
        st_xpos(p, "QT", "QT2", "c_xq512c", "c_xq512s",
                f"l3f{i}_qsc", f"l3f{i}_qscs", 512)
        st_xpos(p, "KT", "KT2", "c_xk512c", "c_xk512s",
                f"l3f{i}_ksc", f"l3f{i}_kscs", 512)
        if upto < 4 + 10 * i + 3:
            return
        st_attention(p, 8, "QT2", "KT2", "VT", "OT", kvmap=lambda h: h,
                     tiles_fn=lambda c: tiles_window(c, 8, 5120),
                     scale=8.0, biasC=20.0, clamp=8.0)
        if upto < 4 + 10 * i + 4:
            return
        st_linear(p, ("OT", 0), 512, (f"l3f{i}_wo", 0, 0), DM, ("AT", 0),
                  dst_dtype=F32)
        st_add(p, "XNF", ["AT"])
        if upto < 4 + 10 * i + 5:
            return
        st_layernorm(p, "XNF", ("SB1", 0), gain=f"l3f{i}_lng",
                     beta=f"l3f{i}_lnb", shifted=True)
        if upto < 4 + 10 * i + 6:
            return
        st_linear(p, ("SB1", 0), DM, (f"l3f{i}_w1p", 0, 0), 5504, ("FH", 0),
                  evac="glu", bias=f"l3f{i}_b1p")
        if upto < 4 + 10 * i + 7:
            return
        st_linear(p, ("FH", 0), 2752, (f"l3f{i}_w2p", 0, 0), DM, ("AT", 0),
                  dst_dtype=F32, bias=f"l3f{i}_b2")
        st_add(p, "XNF", ["AT"])
    if upto < 99:
        return
    st_add(p, "XOUT", ["XNF"])
